# version 04 (final)
# baseline (speedup 1.0000x reference)
import contextlib
import os
import numpy as np
import concourse.bass as bass
import concourse.mybir as mybir
from concourse.bass_utils import run_bass_kernel_spmd

F32 = mybir.dt.float32
BF16 = mybir.dt.bfloat16
AF = mybir.ActivationFunctionType
ALU = mybir.AluOpType
AX = mybir.AxisListType

T = 2048
D = 1024
NCH = 16
H = 16
DH = 4096
NQ = 4
NKD = 8
EPS = 1e-5


class Prog:
    ENGS = ("t", "v", "s", "g", "y")

    def __init__(self):
        self.items = {e: [] for e in self.ENGS}
        self._phase_bar = None
        self._phase_first = set()
        self.slots = {}

    def op(self, eng, fn, waits=(), dma=False, slot=None, sw=()):
        waits = [w for w in waits if w is not None]
        sw = [w for w in sw if w is not None]
        for t in sw:
            assert t[0] == eng, "sw tokens must be same-engine"
        if self._phase_bar is not None and eng not in self._phase_first:
            self._phase_first.add(eng)
            waits = waits + self._phase_bar
        lst = self.items[eng]
        lst.append({"fn": fn, "waits": waits, "dma": dma, "slot": slot,
                    "sw": sw})
        if slot is not None:
            assert dma
            self.slots.setdefault(slot, []).append((eng, len(lst) - 1))
        return (eng, len(lst) - 1)

    def em(self, eng, meth, *args, waits=(), dma=False, slot=None, sw=(), **kw):
        def fn(e, _m=meth, _a=args, _k=kw):
            return getattr(e, _m)(*_a, **_k)
        return self.op(eng, fn, waits=waits, dma=dma, slot=slot, sw=sw)

    def bar(self):
        out = [(e, len(lst) - 1) for e, lst in self.items.items() if lst]
        for sl, toks in self.slots.items():
            out.append(toks[-1])
        return out

    def new_phase(self):
        self._phase_bar = self.bar()
        self._phase_first = set()

    def finalize(self, nc, block, es):
        sems = {e: es.enter_context(nc.semaphore("sem_" + e)) for e in self.ENGS}
        slot_sems = {sl: es.enter_context(nc.semaphore("sl_" + sl))
                     for sl in self.slots}
        slot_of = {}
        slot_cum = {}
        for sl, toks in self.slots.items():
            for n, tok in enumerate(toks):
                slot_of[tok] = sl
                slot_cum[tok] = 16 * (n + 1)
        waited = set()
        for lst in self.items.values():
            for it in lst:
                waited.update(it["waits"])
                waited.update(it["sw"])
        cum = {}
        marks = {}
        for e, lst in self.items.items():
            c = 0
            for i, it in enumerate(lst):
                if it["slot"] is not None:
                    cum[(e, i)] = c
                    continue
                if it["dma"] or (e, i) in waited:
                    marks[(e, i)] = True
                    c += 16 if it["dma"] else 1
                cum[(e, i)] = c

        def emitter(e):
            def body(eng):
                for i, it in enumerate(self.items[e]):
                    for tok in it["sw"]:
                        eng.wait_ge(sems[e], cum[tok])
                    bysem = {}
                    for tok in it["waits"]:
                        if tok in slot_of:
                            key = ("S", slot_of[tok])
                            v = slot_cum[tok]
                        else:
                            we, wi = tok
                            if we == e:
                                continue
                            key = ("E", we)
                            v = cum[tok]
                        if v > bysem.get(key, 0):
                            bysem[key] = v
                    for key, v in bysem.items():
                        sem = slot_sems[key[1]] if key[0] == "S" else sems[key[1]]
                        eng.wait_ge(sem, v)
                    inst = it["fn"](eng)
                    if it["slot"] is not None:
                        inst.then_inc(slot_sems[it["slot"]], 16)
                    elif marks.get((e, i)):
                        inst.then_inc(sems[e], 16 if it["dma"] else 1)
            return body

        block.tensor(emitter("t"))
        block.vector(emitter("v"))
        block.scalar(emitter("s"))
        block.gpsimd(emitter("g"))
        block.sync(emitter("y"))


def build_nc():
    nc = bass.Bass("TRN2", target_bir_lowering=False)
    P = Prog()

    I = {}

    def din(name, shape):
        I[name] = nc.dram_tensor(name, list(shape), F32, kind="ExternalInput").ap()

    din("x", (T, D))
    for w in ("Wq", "Wk", "Wv", "Wo"):
        din(w, (D, D))
    for b in ("bq", "bk", "bv", "bo"):
        din(b, (D,))
    din("Wa", (D, 3 * D))
    din("ba", (3 * D,))
    din("Wp", (D, D))
    din("bp", (D,))
    din("w1T", (D, DH))
    din("w2T", (D, DH))
    din("h1T", (DH, DH))
    din("h2T", (DH, DH))
    din("w3T", (DH, D))
    for s in ("s1c", "s2c", "h1c", "h2c"):
        din(s, (128, 32))
    din("s3r", (D,))
    din("c_eye", (128, 128))
    din("c_tri", (1024,))
    din("c_dmask", (128, 128))
    din("c_ones", (512,))
    OUT = nc.dram_tensor("y", [T, D], F32, kind="ExternalOutput").ap()
    DBG = None
    if os.environ.get("KDBG_CAP"):
        DBG = nc.dram_tensor("dbg", [128, 4096], F32, kind="ExternalOutput").ap()

    es = contextlib.ExitStack()
    with es:
        XR = es.enter_context(nc.sbuf_tensor("XR", [128, NCH * D], F32))
        BIG1 = es.enter_context(nc.sbuf_tensor("BIG1", [128, 16384], BF16))
        BIG2 = es.enter_context(nc.sbuf_tensor("BIG2", [128, 16384], BF16))
        BIG3 = es.enter_context(nc.sbuf_tensor("BIG3", [128, 16384], BF16))
        MED2 = es.enter_context(nc.sbuf_tensor("MED2", [128, 8192], BF16))
        SCR = es.enter_context(nc.sbuf_tensor("SCR", [128, 3584], F32))
        eyeb = es.enter_context(nc.sbuf_tensor("eyeb", [128, 128], BF16))
        dmask = es.enter_context(nc.sbuf_tensor("dmask", [128, 128], F32))
        onesb = es.enter_context(nc.sbuf_tensor("onesb", [1, 512], BF16))
        BROW = es.enter_context(nc.sbuf_tensor("BROW", [1, 4096], BF16))
        trib = BIG3[:, 15360:16384]
        bqr = BROW[:, 0:1024]
        bkr = BROW[:, 1024:2048]
        bvr = BROW[:, 2048:3072]
        bor = BROW[:, 3072:4096]
        bar_ = BROW[:, 0:3072]
        bpr = BROW[:, 3072:4096]
        s1t = es.enter_context(nc.sbuf_tensor("s1t", [128, 32], F32))
        s2t = es.enter_context(nc.sbuf_tensor("s2t", [128, 32], F32))
        h1t = es.enter_context(nc.sbuf_tensor("h1t", [128, 32], F32))
        h2t = es.enter_context(nc.sbuf_tensor("h2t", [128, 32], F32))
        s3bc = es.enter_context(nc.sbuf_tensor("s3bc", [128, 1024], F32))
        lnS = es.enter_context(nc.sbuf_tensor("lnS", [128, 1], F32))
        muT = es.enter_context(nc.sbuf_tensor("muT", [128, 1], F32))
        ssq = es.enter_context(nc.sbuf_tensor("ssq", [128, 1], F32))
        sdT = es.enter_context(nc.sbuf_tensor("sdT", [128, 1], F32))
        rstd = es.enter_context(nc.sbuf_tensor("rstd", [128, 1], F32))
        aqT = es.enter_context(nc.sbuf_tensor("aqT", [128, 16], F32))
        nakT = es.enter_context(nc.sbuf_tensor("nakT", [128, 16], F32))
        negmb = es.enter_context(nc.sbuf_tensor("negmb", [128, 16], F32))
        t32a = es.enter_context(nc.sbuf_tensor("t32a", [128, 32], F32))
        t32b = es.enter_context(nc.sbuf_tensor("t32b", [128, 32], F32))
        Db = es.enter_context(nc.sbuf_tensor("Db", [128, 64], F32))
        Nb = es.enter_context(nc.sbuf_tensor("Nb", [128, 64], F32))
        rDb = es.enter_context(nc.sbuf_tensor("rDb", [128, 64], F32))
        nmT = es.enter_context(nc.sbuf_tensor("nmT", [128, 1], F32))
        ZT = es.enter_context(nc.sbuf_tensor("ZT", [128, 1], F32))
        rZT = es.enter_context(nc.sbuf_tensor("rZT", [128, 1], F32))
        varT = es.enter_context(nc.sbuf_tensor("varT", [128, 1], F32))
        ssq2 = es.enter_context(nc.sbuf_tensor("ssq2", [128, 1], F32))
        ZT2 = es.enter_context(nc.sbuf_tensor("ZT2", [128, 1], F32))

        block = es.enter_context(nc.Block())

        tok_x = []
        for c in range(NCH):
            tok_x.append(P.em("y", "dma_start",
                              out=XR[:, c * D:(c + 1) * D],
                              in_=I["x"][c * 128:(c + 1) * 128, :], dma=True))
        for nm, dst in (("Wq", BIG1[:, 0:8192]), ("Wk", BIG1[:, 8192:16384]),
                        ("Wv", BIG2[:, 0:8192]), ("Wo", BIG2[:, 8192:16384])):
            P.em("g", "dma_start",
                 out=dst.rearrange("p (c n) -> p c n", c=8),
                 in_=I[nm].rearrange("(c p) n -> p c n", p=128), dma=True)
        P.em("g", "dma_start", out=eyeb[:, :], in_=I["c_eye"], dma=True)
        t_trif = P.em("y", "dma_start", out=s3bc[:, :],
                      in_=I["c_tri"].rearrange("(a n) -> a n", a=1)
                      .partition_broadcast(128).squeeze(1), dma=True)
        tok_tri = P.em("v", "tensor_copy", out=trib[:, :], in_=s3bc[:, :],
                       waits=[t_trif])
        P.em("y", "dma_start", out=s3bc[:, :],
             in_=I["s3r"].rearrange("(a n) -> a n", a=1)
             .partition_broadcast(128).squeeze(1),
             dma=True, waits=[tok_tri])
        P.em("y", "dma_start", out=dmask[:, :], in_=I["c_dmask"], dma=True)
        P.em("g", "dma_start", out=onesb[:, :],
             in_=I["c_ones"].rearrange("(a n) -> a n", a=1), dma=True)
        for nm, dst in (("bq", bqr), ("bk", bkr), ("bv", bvr), ("bo", bor)):
            P.em("g", "dma_start", out=dst[:, :],
                 in_=I[nm].rearrange("(a n) -> a n", a=1), dma=True)
        for nm, dst in (("s1c", s1t), ("s2c", s2t), ("h1c", h1t), ("h2c", h2t)):
            P.em("y", "dma_start", out=dst[:, :], in_=I[nm], dma=True)

        trib3 = trib[:, :].rearrange("p (i j) -> p i j", i=32)

        def layernorm(c, xn_out_view, xn_tm, ps0, ps1, ident,
                      extra_wait=(), tp_wait=()):
            x_c = XR[:, c * D:(c + 1) * D]
            xc = SCR[:, 0:1024]
            sq = SCR[:, 1024:2048]
            t_ls = P.em("v", "reduce_sum", lnS[:, :], x_c, AX.X,
                        waits=list(extra_wait))
            t_mu = P.em("v", "tensor_scalar_mul", muT[:, :], lnS[:, :], 1.0 / D,
                        sw=[t_ls])
            t_xc = P.em("v", "tensor_scalar", out=xc, in0=x_c, scalar1=muT[:, :],
                        scalar2=None, op0=ALU.subtract, sw=[t_mu])
            t_sq = P.em("s", "activation", sq, xc, AF.Square, waits=[t_xc])
            t_ssq = P.em("v", "reduce_sum", ssq[:, :], sq, AX.X, waits=[t_sq])
            t_var = P.em("v", "tensor_scalar", out=varT[:, :], in0=ssq[:, :],
                         scalar1=1.0 / D, scalar2=EPS, op0=ALU.mult,
                         op1=ALU.add, sw=[t_ssq])
            t_sd = P.em("s", "activation", sdT[:, :], varT[:, :], AF.Sqrt,
                        waits=[t_var])
            t_rs = P.em("v", "reciprocal", rstd[:, :], sdT[:, :], waits=[t_sd])
            t_xn = P.em("v", "tensor_scalar", out=xn_tm, in0=xc,
                        scalar1=rstd[:, :], scalar2=None, op0=ALU.mult,
                        sw=[t_rs])
            tts = []
            for d in range(8):
                ps = ps0 if d < 4 else ps1
                w = []
                if d == 0:
                    w = [t_xn] + list(tp_wait)
                tts.append(P.em("t", "transpose",
                                ps[:, (d % 4) * 128:(d % 4 + 1) * 128],
                                xn_tm[:, d * 128:(d + 1) * 128], ident,
                                waits=w))
            e0 = P.em("s", "copy", xn_out_view[:, 0:4, :],
                      ps0[:, :].rearrange("p (a n) -> p a n", a=4),
                      waits=[tts[3]])
            e1 = P.em("s", "copy", xn_out_view[:, 4:8, :],
                      ps1[:, :].rearrange("p (a n) -> p a n", a=4),
                      waits=[tts[7]])
            return e1, tts[7]

        def rect_views(buf):
            b3 = buf.rearrange("p (r i j) -> p r i j", r=3, i=32)
            return b3[:, 0, :, :], b3[:, 1, :, :], b3[:, 2, :, :]

        P.new_phase()
        stage1_resid = [None] * NCH
        with contextlib.ExitStack() as s1es:
            psA = s1es.enter_context(nc.psum_tensor("psA", [128, 1024], F32))
            psB = s1es.enter_context(nc.psum_tensor("psB", [128, 1024], F32))
            psC = s1es.enter_context(nc.psum_tensor("psC", [128, 1024], F32))
            psL0 = s1es.enter_context(nc.psum_tensor("psL0", [128, 512], BF16))
            psL1 = s1es.enter_context(nc.psum_tensor("psL1", [128, 512], BF16))

            Wq_b = BIG1[:, 0:8192]
            Wk_b = BIG1[:, 8192:16384]
            Wv_b = BIG2[:, 0:8192]
            Wo_b = BIG2[:, 8192:16384]
            S_buf = [BIG3[:, 0:3072], BIG3[:, 3072:6144]]
            E_buf = [BIG3[:, 6144:9216], BIG3[:, 9216:12288]]
            Evb = BIG3[:, 12288:15360]
            q_tm = MED2[:, 0:1024]
            k_tm = MED2[:, 1024:2048]
            v_tm = MED2[:, 2048:3072]
            attnb = MED2[:, 3072:4096]
            attn_fm = MED2[:, 4096:5120]
            xn1 = MED2[:, 5120:6144]
            xn_tm1 = MED2[:, 6144:7168]
            xn1v = xn1.rearrange("p (c n) -> p c n", c=8)
            afmv = attn_fm.rearrange("p (c n) -> p c n", c=8)

            pv = {"evq": None, "mm_qkv": None, "resid_prev": None,
                  "atp_evac": None, "atp_last": None, "exp_last": None}
            for c in range(NCH):
                lnw = [tok_x[c]]
                if pv["mm_qkv"] is not None:
                    lnw.append(pv["mm_qkv"])
                if c == 1 and pv.get("dbg_toks"):
                    lnw += pv["dbg_toks"]
                tpw = [pv["atp_evac"]] if pv["atp_evac"] is not None else []
                t_lnev, _ = layernorm(c, xn1v, xn_tm1, psL0, psL1, eyeb[:, :],
                                      extra_wait=lnw, tp_wait=tpw)

                mmw = [t_lnev]
                if pv["evq"] is not None:
                    mmw.append(pv["evq"])
                if pv["resid_prev"] is not None:
                    mmw.append(pv["resid_prev"])
                first = True
                for ps, brow in ((psA, bqr), (psB, bkr), (psC, bvr)):
                    for half in range(2):
                        P.em("t", "matmul", ps[:, half * 512:(half + 1) * 512],
                             onesb[0:1, 0:128],
                             brow[0:1, half * 512:(half + 1) * 512],
                             start=True, stop=False,
                             waits=mmw if first else [])
                        first = False
                last_mm = None
                for ps, wb in ((psA, Wq_b), (psB, Wk_b), (psC, Wv_b)):
                    for k in range(NKD):
                        for half in range(2):
                            last_mm = P.em(
                                "t", "matmul",
                                ps[:, half * 512:(half + 1) * 512],
                                xn1[:, k * 128:(k + 1) * 128],
                                wb[:, k * 1024 + half * 512:
                                   k * 1024 + (half + 1) * 512],
                                start=False, stop=(k == NKD - 1))
                pv["mm_qkv"] = last_mm
                t_evq = P.em("v", "tensor_scalar", out=q_tm, in0=psA[:, :],
                             scalar1=0.125, scalar2=None, op0=ALU.mult,
                             waits=[last_mm])
                P.em("v", "tensor_copy", out=k_tm, in_=psB[:, :])
                t_evv = P.em("v", "tensor_copy", out=v_tm, in_=psC[:, :])
                pv["evq"] = t_evv

                q3 = q_tm.rearrange("p (h i) -> p h i", h=H)
                k3 = k_tm.rearrange("p (h i) -> p h i", h=H)
                v3 = v_tm.rearrange("p (h i) -> p h i", h=H)
                bw = [pv["exp_last"]] if pv["exp_last"] is not None else []
                P.em("v", "tensor_reduce", aqT[:, :], q3, AX.X, ALU.max,
                     apply_absolute_value=True, waits=bw)
                t_nk = P.em("v", "tensor_reduce", nakT[:, :], k3, AX.X, ALU.max,
                            apply_absolute_value=True, negate=True)
                P.em("v", "tensor_mul", negmb[:, :], aqT[:, :], nakT[:, :],
                     sw=[t_nk])

                expt = [None] * (H + 2)
                vdone = [None] * (H + 2)
                expt[0] = pv.get("expt16")
                expt[1] = pv.get("expt17")
                vdone[0] = pv.get("vdone16")
                vdone[1] = pv.get("vdone17")
                t_at = None
                for h in range(H + 1):
                    if h < H:
                        par = h % 2
                        S = S_buf[par]
                        R1, R2, R3 = rect_views(S)
                        qh = q3[:, h, :]
                        kh = k3[:, h, :]
                        w = [expt[h]] if expt[h] is not None else []
                        P.em("v", "tensor_tensor", out=R1,
                             in0=qh[:, 32:64].unsqueeze(2).broadcast_to((128, 32, 32)),
                             in1=kh[:, 0:32].unsqueeze(1).broadcast_to((128, 32, 32)),
                             op=ALU.mult, waits=w)
                        P.em("v", "tensor_tensor", out=R2,
                             in0=qh[:, 0:32].unsqueeze(2).broadcast_to((128, 32, 32)),
                             in1=kh[:, 0:32].unsqueeze(1).broadcast_to((128, 32, 32)),
                             op=ALU.mult)
                        t_sb = P.em("v", "tensor_tensor", out=R3,
                                    in0=qh[:, 32:64].unsqueeze(2).broadcast_to((128, 32, 32)),
                                    in1=kh[:, 32:64].unsqueeze(1).broadcast_to((128, 32, 32)),
                                    op=ALU.mult)
                        ew = [t_sb]
                        if vdone[h] is not None:
                            ew.append(vdone[h])
                        expt[h + 2] = P.em("s", "activation", E_buf[par][:, :],
                                           S[:, :], AF.Exp,
                                           bias=negmb[:, h:h + 1], waits=ew)
                    if h >= 1:
                        hh = h - 1
                        par = hh % 2
                        E = E_buf[par]
                        E1, E2, E3v = rect_views(E)
                        V1, V2, V3v = rect_views(Evb)
                        vh = v3[:, hh, :]
                        te = expt[hh + 2]
                        mw0 = [te]
                        if hh == 0 and pv["atp_last"] is not None:
                            mw0.append(pv["atp_last"])
                        P.em("v", "tensor_tensor", out=E2, in0=E2, in1=trib3,
                             op=ALU.mult, waits=mw0)
                        P.em("v", "tensor_tensor", out=E3v, in0=E3v, in1=trib3,
                             op=ALU.mult)
                        P.em("v", "tensor_tensor", out=V1, in0=E1,
                             in1=vh[:, 0:32].unsqueeze(1).broadcast_to((128, 32, 32)),
                             op=ALU.mult)
                        P.em("v", "tensor_tensor", out=V2, in0=E2,
                             in1=vh[:, 0:32].unsqueeze(1).broadcast_to((128, 32, 32)),
                             op=ALU.mult)
                        P.em("v", "tensor_tensor", out=V3v, in0=E3v,
                             in1=vh[:, 32:64].unsqueeze(1).broadcast_to((128, 32, 32)),
                             op=ALU.mult)
                        P.em("v", "tensor_reduce", Db[:, 0:32], E2, AX.X, ALU.add)
                        P.em("v", "tensor_reduce", t32a[:, :], E1, AX.X, ALU.add)
                        t_b1 = P.em("v", "tensor_reduce", t32b[:, :], E3v,
                                    AX.X, ALU.add)
                        P.em("v", "tensor_add", Db[:, 32:64], t32a[:, :],
                             t32b[:, :], sw=[t_b1])
                        P.em("v", "tensor_reduce", Nb[:, 0:32], V2, AX.X, ALU.add)
                        P.em("v", "tensor_reduce", t32a[:, :], V1, AX.X, ALU.add)
                        t_b2 = P.em("v", "tensor_reduce", t32b[:, :], V3v,
                                    AX.X, ALU.add)
                        t_na = P.em("v", "tensor_add", Nb[:, 32:64], t32a[:, :],
                                    t32b[:, :], sw=[t_b2])
                        t_rd = P.em("v", "reciprocal", rDb[:, :], Db[:, :])
                        t_at = P.em("v", "tensor_mul",
                                    attnb[:, hh * 64:(hh + 1) * 64],
                                    Nb[:, :], rDb[:, :], sw=[t_na, t_rd])
                        vdone[hh + 2] = t_at
                if DBG is not None and c == 0:
                    dbg_toks = []
                    for ii, (srcap, dst) in enumerate([
                            (xn_tm1[:, :], DBG[:, 0:1024]),
                            (q_tm[:, :], DBG[:, 1024:2048]),
                            (xn1[:, :], DBG[:, 2048:3072]),
                            (v_tm[:, :], DBG[:, 3072:4096])]):
                        dbg_toks.append(P.em("g", "dma_start", out=dst,
                                             in_=srcap, dma=True,
                                             waits=[t_at], slot="dbgc"))
                    pv["dbg_toks"] = dbg_toks
                pv["exp_last"] = expt[H + 1]
                pv["expt16"] = expt[H]
                pv["expt17"] = expt[H + 1]
                pv["vdone16"] = vdone[H]
                pv["vdone17"] = vdone[H + 1]

                atp = []
                for d in range(8):
                    ps = psL0 if d < 4 else psL1
                    w = [t_at, t_lnev] if d == 0 else []
                    atp.append(P.em("t", "transpose",
                                    ps[:, (d % 4) * 128:(d % 4 + 1) * 128],
                                    attnb[:, d * 128:(d + 1) * 128],
                                    eyeb[:, :], waits=w))
                pv["atp_last"] = atp[7]
                P.em("s", "copy", afmv[:, 0:4, :],
                     psL0[:, :].rearrange("p (a n) -> p a n", a=4),
                     waits=[atp[3]])
                a1 = P.em("s", "copy", afmv[:, 4:8, :],
                          psL1[:, :].rearrange("p (a n) -> p a n", a=4),
                          waits=[atp[7]])
                pv["atp_evac"] = a1

                wo_w = [a1, t_evq]
                if pv["resid_prev"] is not None:
                    wo_w.append(pv["resid_prev"])
                for half in range(2):
                    P.em("t", "matmul", psA[:, half * 512:(half + 1) * 512],
                         onesb[0:1, 0:128], bor[0:1, half * 512:(half + 1) * 512],
                         start=True, stop=False,
                         waits=wo_w if half == 0 else [])
                lm = None
                for k in range(NKD):
                    for half in range(2):
                        lm = P.em("t", "matmul",
                                  psA[:, half * 512:(half + 1) * 512],
                                  attn_fm[:, k * 128:(k + 1) * 128],
                                  Wo_b[:, k * 1024 + half * 512:
                                       k * 1024 + (half + 1) * 512],
                                  start=False, stop=(k == NKD - 1))
                stage1_resid[c] = P.em("v", "tensor_add",
                                       XR[:, c * D:(c + 1) * D],
                                       XR[:, c * D:(c + 1) * D],
                                       psA[:, :], waits=[lm])
                pv["resid_prev"] = stage1_resid[c]

        def ffn_pass(tok_xin, tag):
            h_buf = BIG1
            g1_buf = BIG2
            w1s = [BIG3[:, 0:4096], BIG3[:, 8192:12288]]
            w2s = [BIG3[:, 4096:8192], BIG3[:, 12288:16384]]
            xn_q = MED2[:, 0:4096]
            ring = [MED2[:, 4096 + i * 512: 4096 + (i + 1) * 512] for i in range(4)]
            w3s = [MED2[:, 6144:7168], MED2[:, 7168:8192]]
            silu_b = SCR[:, 3072:3328].bitcast(BF16)
            t2_b = SCR[:, 3328:3584].bitcast(BF16)
            xn_tm2 = SCR[:, 2048:2560].bitcast(BF16)
            w3tmp = SCR[:, 2048:3072]
            tok_xout = [None] * NCH
            P.new_phase()
            L = {"slab_mm": [None, None], "silu": [None, None],
                 "t2c": [None, None], "gelu": [None] * 4,
                 "ring_mm": [None] * 4, "w3ev_last": None,
                 "w3_dma_mm": [None, None], "hmul_last": None,
                 "lnev": None, "w3mm_last": None}

            def qbar():
                out = [L["w3ev_last"], L["w3mm_last"],
                       L["silu"][0], L["silu"][1], L["t2c"][0], L["t2c"][1]]
                out += L["gelu"]
                return [t for t in out if t is not None]

            for q in range(NQ):
                xnqv8 = xn_q.rearrange("p (d n) -> p d n", d=8)
                with contextlib.ExitStack() as qes:
                    pt1 = [qes.enter_context(nc.psum_tensor(
                        f"pt1{tag}{q}{i}", [128, 512], F32)) for i in range(2)]
                    pt2 = [qes.enter_context(nc.psum_tensor(
                        f"pt2{tag}{q}{i}", [128, 512], F32)) for i in range(2)]
                    psL2 = [qes.enter_context(nc.psum_tensor(
                        f"psL2{tag}{q}{i}", [128, 512], BF16)) for i in range(2)]
                    for ci in range(4):
                        c = q * 4 + ci
                        lnw = [tok_xin[c]]
                        if L["slab_mm"][0] is not None:
                            lnw.append(L["slab_mm"][0])
                        if L["slab_mm"][1] is not None:
                            lnw.append(L["slab_mm"][1])
                        tpw = qbar()
                        if L["lnev"] is not None:
                            tpw.append(L["lnev"])
                        ev, _ = layernorm(
                            c, xnqv8[:, :, ci * 128:(ci + 1) * 128],
                            xn_tm2, psL2[0], psL2[1], eyeb[:, :],
                            extra_wait=lnw, tp_wait=tpw)
                        L["lnev"] = ev
                    for og in range(8):
                        par = og % 2
                        dw = []
                        if L["slab_mm"][par] is not None:
                            dw.append(L["slab_mm"][par])
                        P.em("g", "dma_start",
                             out=w1s[par].rearrange("p (c n) -> p c n", c=8),
                             in_=I["w1T"][:, og * 512:(og + 1) * 512].rearrange(
                                 "(c p) n -> p c n", p=128),
                             dma=True, waits=dw, slot=f"slab{par}")
                        d2 = P.em("g", "dma_start",
                                  out=w2s[par].rearrange("p (c n) -> p c n", c=8),
                                  in_=I["w2T"][:, og * 512:(og + 1) * 512].rearrange(
                                      "(c p) n -> p c n", p=128),
                                  dma=True, slot=f"slab{par}")
                        for o_sub in range(4):
                            o = og * 4 + o_sub
                            pp = o % 2
                            mw = [d2, L["lnev"]]
                            if og == 0 and o_sub == 0:
                                mw += qbar()
                            if L["silu"][pp] is not None:
                                mw.append(L["silu"][pp])
                            if L["t2c"][pp] is not None:
                                mw.append(L["t2c"][pp])
                            lm1 = lm2 = None
                            for k in range(NKD):
                                lm1 = P.em("t", "matmul", pt1[pp][:, :],
                                           w1s[par][:, k * 512 + o_sub * 128:
                                                    k * 512 + (o_sub + 1) * 128],
                                           xn_q[:, k * 512:(k + 1) * 512],
                                           start=(k == 0), stop=(k == NKD - 1),
                                           waits=mw if k == 0 else [])
                            for k in range(NKD):
                                lm2 = P.em("t", "matmul", pt2[pp][:, :],
                                           w2s[par][:, k * 512 + o_sub * 128:
                                                    k * 512 + (o_sub + 1) * 128],
                                           xn_q[:, k * 512:(k + 1) * 512],
                                           start=(k == 0), stop=(k == NKD - 1))
                            L["slab_mm"][par] = lm2
                            ts1 = P.em("s", "activation", silu_b[:, :],
                                       pt1[pp][:, :], AF.Silu,
                                       scale=s1t[:, o:o + 1],
                                       waits=[lm1, L["hmul_last"]])
                            ts2 = P.em("s", "activation", t2_b[:, :],
                                       pt2[pp][:, :], AF.Copy,
                                       scale=s2t[:, o:o + 1], waits=[lm2])
                            L["silu"][pp] = ts1
                            L["t2c"][pp] = ts2
                            hw = [ts2]
                            if L["w3mm_last"] is not None:
                                hw.append(L["w3mm_last"])
                            L["hmul_last"] = P.em(
                                "v", "tensor_tensor",
                                out=h_buf[:, o * 512:(o + 1) * 512],
                                in0=silu_b[:, :], in1=t2_b[:, :],
                                op=ALU.mult, waits=hw)
                with contextlib.ExitStack() as qes:
                    pg = [qes.enter_context(nc.psum_tensor(
                        f"pg{tag}{q}{i}", [128, 512], F32)) for i in range(4)]
                    h1_mm_last = None
                    for o2g in range(8):
                        for k in range(32):
                            r = k % 4
                            dw = []
                            if L["ring_mm"][r] is not None:
                                dw.append(L["ring_mm"][r])
                            dd = P.em("g", "dma_start", out=ring[r][:, :],
                                      in_=I["h1T"][k * 128:(k + 1) * 128,
                                                   o2g * 512:(o2g + 1) * 512],
                                      dma=True, waits=dw, slot=f"ring{r}")
                            for j in range(4):
                                mw = [dd]
                                if k == 0:
                                    mw.append(L["hmul_last"])
                                    if L["gelu"][j] is not None:
                                        mw.append(L["gelu"][j])
                                    if o2g == 0 and j == 0:
                                        mw += [L["silu"][0], L["silu"][1],
                                               L["t2c"][0], L["t2c"][1],
                                               L["lnev"]]
                                h1_mm_last = P.em(
                                    "t", "matmul", pg[j][:, :],
                                    ring[r][:, j * 128:(j + 1) * 128],
                                    h_buf[:, k * 512:(k + 1) * 512],
                                    start=(k == 0), stop=(k == 31), waits=mw)
                            L["ring_mm"][r] = h1_mm_last
                        for j in range(4):
                            o2 = o2g * 4 + j
                            L["gelu"][j] = P.em(
                                "s", "activation",
                                g1_buf[:, o2 * 512:(o2 + 1) * 512],
                                pg[j][:, :], AF.Gelu, scale=h1t[:, o2:o2 + 1],
                                waits=[h1_mm_last] if j == 0 else [])
                    h1_gelu_last = L["gelu"][3]
                    h1_mm_all = h1_mm_last
                    for o2g in range(8):
                        for k in range(32):
                            r = k % 4
                            dw = []
                            if L["ring_mm"][r] is not None:
                                dw.append(L["ring_mm"][r])
                            dd = P.em("g", "dma_start", out=ring[r][:, :],
                                      in_=I["h2T"][k * 128:(k + 1) * 128,
                                                   o2g * 512:(o2g + 1) * 512],
                                      dma=True, waits=dw, slot=f"ring{r}")
                            for j in range(4):
                                mw = [dd]
                                if k == 0:
                                    mw.append(h1_gelu_last)
                                    if L["gelu"][j] is not None:
                                        mw.append(L["gelu"][j])
                                hm = P.em(
                                    "t", "matmul", pg[j][:, :],
                                    ring[r][:, j * 128:(j + 1) * 128],
                                    g1_buf[:, k * 512:(k + 1) * 512],
                                    start=(k == 0), stop=(k == 31), waits=mw)
                            L["ring_mm"][r] = hm
                        for j in range(4):
                            o2 = o2g * 4 + j
                            gw = [L["ring_mm"][3]] if j == 0 else []
                            if o2g == 0 and j == 0:
                                gw.append(h1_mm_all)
                            L["gelu"][j] = P.em(
                                "s", "activation",
                                h_buf[:, o2 * 512:(o2 + 1) * 512],
                                pg[j][:, :], AF.Gelu, scale=h2t[:, o2:o2 + 1],
                                waits=gw)
                    h2_gelu_last = L["gelu"][3]
                with contextlib.ExitStack() as qes:
                    pw3 = [qes.enter_context(nc.psum_tensor(
                        f"pw3{tag}{q}{i}", [128, 1024], F32)) for i in range(4)]
                    w3mm_last = None
                    for k in range(32):
                        par = k % 2
                        dw = []
                        if L["w3_dma_mm"][par] is not None:
                            dw.append(L["w3_dma_mm"][par])
                        dd = P.em("g", "dma_start", out=w3s[par][:, :],
                                  in_=I["w3T"][k * 128:(k + 1) * 128, :],
                                  dma=True, waits=dw, slot=f"w3{par}")
                        for tc in range(4):
                            for half in range(2):
                                mw = [dd]
                                if k == 0 and tc == 0 and half == 0:
                                    mw.append(h2_gelu_last)
                                    if L["w3ev_last"] is not None:
                                        mw.append(L["w3ev_last"])
                                w3mm_last = P.em(
                                    "t", "matmul",
                                    pw3[tc][:, half * 512:(half + 1) * 512],
                                    h_buf[:, k * 512 + tc * 128:
                                          k * 512 + (tc + 1) * 128],
                                    w3s[par][:, half * 512:(half + 1) * 512],
                                    start=(k == 0), stop=(k == 31), waits=mw)
                        L["w3_dma_mm"][par] = w3mm_last
                    L["w3mm_last"] = w3mm_last
                    for tc in range(4):
                        c = q * 4 + tc
                        P.em("v", "tensor_mul", w3tmp, pw3[tc][:, :],
                             s3bc[:, :], waits=[w3mm_last])
                        tok_xout[c] = P.em("v", "tensor_add",
                                           XR[:, c * D:(c + 1) * D],
                                           XR[:, c * D:(c + 1) * D], w3tmp)
                    L["w3ev_last"] = tok_xout[q * 4 + 3]
            return tok_xout

        if int(os.environ.get("KDBG_STAGES", "4")) >= 2:
            tok_x2 = ffn_pass(stage1_resid, "a")
        else:
            tok_x2 = stage1_resid

        def attn_b(tok_xin):
            Qb = BIG1
            Kb = BIG2
            Vb = BIG3
            xn3 = MED2[:, 0:4096]
            xn3v8 = xn3.rearrange("p (d n) -> p d n", d=8)
            qkslab = [MED2[:, 4096:5120], MED2[:, 5120:6144]]
            vstage = MED2[:, 6144:6656]
            xn_tm3 = MED2[:, 7168:8192]
            E0 = SCR[:, 0:2048]
            tok_xout = [None] * NCH
            P.new_phase()
            t_ba = P.em("g", "dma_start", out=bar_[:, :],
                        in_=I["ba"].rearrange("(a n) -> a n", a=1), dma=True,
                        slot="brow")
            t_bp = P.em("g", "dma_start", out=bpr[:, :],
                        in_=I["bp"].rearrange("(a n) -> a n", a=1), dma=True,
                        slot="brow")
            with contextlib.ExitStack() as ps_es:
                psqk = [ps_es.enter_context(nc.psum_tensor(
                    f"pqk{i}", [128, 512], F32)) for i in range(2)]
                psL0b = ps_es.enter_context(
                    nc.psum_tensor("psL0b", [128, 512], BF16))
                psL1b = ps_es.enter_context(
                    nc.psum_tensor("psL1b", [128, 512], BF16))
                L = {"slab_mm": [None, None], "ev": [None, None],
                     "lnev": None, "vtp": None, "vtp_ev": None,
                     "qkv_mm_last": None}
                for q in range(NQ):
                    for ci in range(4):
                        c = q * 4 + ci
                        lnw = [tok_xin[c]]
                        if L["qkv_mm_last"] is not None:
                            lnw.append(L["qkv_mm_last"])
                        tpw = []
                        if L["vtp_ev"] is not None:
                            tpw.append(L["vtp_ev"])
                        if L["lnev"] is not None:
                            tpw.append(L["lnev"])
                        ev, _ = layernorm(
                            c, xn3v8[:, :, ci * 128:(ci + 1) * 128],
                            xn_tm3, psL0b, psL1b, eyeb[:, :],
                            extra_wait=lnw, tp_wait=tpw)
                        L["lnev"] = ev
                    for oc in range(24):
                        par = oc % 2
                        dw = []
                        if L["slab_mm"][par] is not None:
                            dw.append(L["slab_mm"][par])
                        dd = P.em("g", "dma_start",
                                  out=qkslab[par].rearrange("p (c n) -> p c n", c=8),
                                  in_=I["Wa"][:, oc * 128:(oc + 1) * 128].rearrange(
                                      "(c p) n -> p c n", p=128),
                                  dma=True, waits=dw, slot=f"slab{par}")
                        pw = [dd, L["lnev"]]
                        if L["ev"][par] is not None:
                            pw.append(L["ev"][par])
                        P.em("t", "matmul", psqk[par][:, :],
                             bar_[0:1, oc * 128:(oc + 1) * 128],
                             onesb[0:1, 0:512],
                             start=True, stop=False, waits=pw + [t_bp])
                        lmm = None
                        for k in range(NKD):
                            lmm = P.em("t", "matmul", psqk[par][:, :],
                                       qkslab[par][:, k * 128:(k + 1) * 128],
                                       xn3[:, k * 512:(k + 1) * 512],
                                       start=False, stop=(k == NKD - 1))
                        L["slab_mm"][par] = lmm
                        L["qkv_mm_last"] = lmm
                        if oc < 8:
                            L["ev"][par] = P.em(
                                "v", "tensor_scalar",
                                out=Qb[:, oc * 2048 + q * 512:
                                       oc * 2048 + (q + 1) * 512],
                                in0=psqk[par][:, :], scalar1=0.125,
                                scalar2=None, op0=ALU.mult, waits=[lmm])
                        elif oc < 16:
                            L["ev"][par] = P.em(
                                "v", "tensor_copy",
                                out=Kb[:, (oc - 8) * 2048 + q * 512:
                                       (oc - 8) * 2048 + (q + 1) * 512],
                                in_=psqk[par][:, :], waits=[lmm])
                        else:
                            vc = oc - 16
                            tv = P.em("v", "tensor_copy", out=vstage[:, :],
                                      in_=psqk[par][:, :],
                                      waits=[lmm] + ([L["vtp"]] if L["vtp"] is not None else []))
                            L["ev"][par] = tv
                            vt = []
                            for tsub in range(4):
                                ps = psL0b if tsub < 2 else psL1b
                                w2 = [tv]
                                if tsub == 0:
                                    if L["vtp_ev"] is not None:
                                        w2.append(L["vtp_ev"])
                                    w2.append(L["lnev"])
                                vt.append(P.em(
                                    "t", "transpose",
                                    ps[:, (tsub % 2) * 128:(tsub % 2 + 1) * 128],
                                    vstage[:, tsub * 128:(tsub + 1) * 128],
                                    eyeb[:, :], waits=w2))
                            L["vtp"] = vt[3]
                            ev2 = None
                            for tsub in range(4):
                                ps = psL0b if tsub < 2 else psL1b
                                tchunk = q * 4 + tsub
                                ev2 = P.em(
                                    "s", "copy",
                                    Vb[:, tchunk * 1024 + vc * 128:
                                       tchunk * 1024 + (vc + 1) * 128],
                                    ps[:, (tsub % 2) * 128:(tsub % 2 + 1) * 128],
                                    waits=[vt[3]])
                            L["vtp_ev"] = ev2
                qkv_done = [L["qkv_mm_last"], L["ev"][0], L["ev"][1],
                            L["vtp_ev"]]

            P.new_phase()
            wpj = [MED2[:, 0:1024], MED2[:, 1024:2048]]
            Oblk = [MED2[:, 4096:5120], MED2[:, 5120:6144]]
            Ebf = MED2[:, 2048:4096]
            ETb = MED2[:, 6144:8192]
            with contextlib.ExitStack() as ps_es:
                psS = ps_es.enter_context(
                    nc.psum_tensor("psS", [128, 2048], F32))
                psET = ps_es.enter_context(
                    nc.psum_tensor("psET", [128, 512], BF16))
                psO = ps_es.enter_context(
                    nc.psum_tensor("psO", [128, 128], F32))
                ppj = ps_es.enter_context(
                    nc.psum_tensor("ppj", [128, 1024], F32))
                A = {"exp": None, "oev": None, "etev": None,
                     "projmm": [None, None], "projev": None,
                     "wp_mm": [None, None], "tp_last": None, "pv_last": None}
                for qtile in range(NCH):
                    kext = (qtile + 1) * 128
                    nkc = qtile + 1
                    opar = qtile % 2
                    for h in range(H):
                        qp = (h % 2) * 64
                        qc = h // 2
                        smw = []
                        if A["exp"] is not None:
                            smw.append(A["exp"])
                        if qtile == 0 and h == 0:
                            smw += qkv_done
                        lsm = None
                        for j in range((kext + 511) // 512):
                            n = min(512, kext - 512 * j)
                            lsm = P.em("t", "matmul",
                                       psS[:, 512 * j:512 * j + n],
                                       Qb[qp:qp + 64,
                                          qc * 2048 + qtile * 128:
                                          qc * 2048 + (qtile + 1) * 128],
                                       Kb[qp:qp + 64,
                                          qc * 2048 + 512 * j:
                                          qc * 2048 + 512 * j + n],
                                       start=True, stop=True,
                                       waits=smw if j == 0 else [])
                        t_msk = P.em("v", "tensor_tensor",
                                     out=psS[:, kext - 128:kext],
                                     in0=psS[:, kext - 128:kext],
                                     in1=dmask[:, :], op=ALU.add, waits=[lsm])
                        t_nm = P.em("v", "tensor_reduce", nmT[:, :],
                                    psS[:, 0:kext], AX.X, ALU.max, negate=True,
                                    sw=[t_msk])
                        t_exp = P.em("s", "activation", E0[:, 0:kext],
                                     psS[:, 0:kext], AF.Exp, bias=nmT[:, :],
                                     accum_out=ZT[:, :], waits=[t_nm])
                        A["exp"] = t_exp
                        t_z2 = P.em("s", "copy", ZT2[:, :], ZT[:, :])
                        t_rz = P.em("v", "reciprocal", rZT[:, :], ZT2[:, :],
                                    waits=[t_z2])
                        ew = []
                        if A["tp_last"] is not None:
                            ew.append(A["tp_last"])
                        t_e = P.em("v", "tensor_scalar", out=Ebf[:, 0:kext],
                                   in0=E0[:, 0:kext], scalar1=rZT[:, :],
                                   scalar2=None, op0=ALU.mult, waits=ew,
                                   sw=[t_rz])
                        lpv = None
                        tps = None
                        for g in range((nkc + 3) // 4):
                            js = list(range(4 * g, min(4 * g + 4, nkc)))
                            tps = []
                            for j in js:
                                w2 = []
                                if j == 4 * g:
                                    w2 = [t_e]
                                    if A["etev"] is not None:
                                        w2.append(A["etev"])
                                tps.append(P.em(
                                    "t", "transpose",
                                    psET[:, (j % 4) * 128:(j % 4 + 1) * 128],
                                    Ebf[:, j * 128:(j + 1) * 128],
                                    eyeb[:, :], waits=w2))
                            wdt = len(js)
                            A["etev"] = P.em(
                                "v", "tensor_copy",
                                out=ETb[:, 4 * g * 128:(4 * g + wdt) * 128],
                                in_=psET[:, 0:wdt * 128], waits=[tps[-1]])
                            for j in js:
                                w2 = [A["etev"]]
                                if j == 0 and A["oev"] is not None:
                                    w2.append(A["oev"])
                                lpv = P.em(
                                    "t", "matmul", psO[0:64, 0:128],
                                    Vb[:, j * 1024 + h * 64:
                                       j * 1024 + (h + 1) * 64],
                                    ETb[:, j * 128:(j + 1) * 128],
                                    start=(j == 0), stop=(j == nkc - 1),
                                    skip_group_check=True, waits=w2)
                        A["tp_last"] = tps[-1]
                        A["pv_last"] = lpv
                        ow = [lpv]
                        if h == 0 and A["projmm"][opar] is not None:
                            ow.append(A["projmm"][opar])
                        A["oev"] = P.em(
                            "s", "copy",
                            Oblk[opar][qp:qp + 64, qc * 128:(qc + 1) * 128],
                            psO[0:64, 0:128], waits=ow)
                    pjw = [A["oev"]]
                    if A["projev"] is not None:
                        pjw.append(A["projev"])
                    for half in range(2):
                        P.em("t", "matmul", ppj[:, half * 512:(half + 1) * 512],
                             onesb[0:1, 0:128],
                             bpr[0:1, half * 512:(half + 1) * 512],
                             start=True, stop=False,
                             waits=pjw if half == 0 else [])
                    lpm = None
                    for k in range(NKD):
                        par = k % 2
                        dw = []
                        if A["wp_mm"][par] is not None:
                            dw.append(A["wp_mm"][par])
                        dd = P.em("g", "dma_start", out=wpj[par][:, :],
                                  in_=I["Wp"][k * 128:(k + 1) * 128, :],
                                  dma=True, waits=dw, slot=f"w3{par}")
                        for half in range(2):
                            lpm = P.em(
                                "t", "matmul",
                                ppj[:, half * 512:(half + 1) * 512],
                                Oblk[opar][:, k * 128:(k + 1) * 128],
                                wpj[par][:, half * 512:(half + 1) * 512],
                                start=False, stop=(k == NKD - 1), waits=[dd])
                        A["wp_mm"][par] = lpm
                    A["projmm"][opar] = lpm
                    c = qtile
                    tok_xout[c] = P.em("v", "tensor_add",
                                       XR[:, c * D:(c + 1) * D],
                                       XR[:, c * D:(c + 1) * D],
                                       ppj[:, :], waits=[lpm])
                    A["projev"] = tok_xout[c]
            return tok_xout

        _dbg = int(os.environ.get("KDBG_STAGES", "4"))
        if _dbg >= 3:
            tok_x3 = attn_b(tok_x2)
        else:
            tok_x3 = tok_x2
        if _dbg >= 4:
            tok_x4 = ffn_pass(tok_x3, "b")
        else:
            tok_x4 = tok_x3

        P.new_phase()
        out_toks = []
        for c in range(NCH):
            out_toks.append(P.em("y", "dma_start",
                                 out=OUT[c * 128:(c + 1) * 128, :],
                                 in_=XR[:, c * D:(c + 1) * D],
                                 dma=True, waits=[tok_x4[c]]))
        P.em("t", "nop", waits=out_toks)

        P.finalize(nc, block, es)
    return nc


_NC_CACHE = None


def kernel(**inputs):
    global _NC_CACHE
    if _NC_CACHE is None:
        _NC_CACHE = build_nc()
    nc = _NC_CACHE

    f32 = np.float32
    x = np.ascontiguousarray(np.asarray(inputs["L0_attn_a_x"], dtype=f32))
    base = {
        "Wq": np.ascontiguousarray(np.asarray(inputs["L0_attn_a_Wq"], f32)),
        "Wk": np.ascontiguousarray(np.asarray(inputs["L0_attn_a_Wk"], f32)),
        "Wv": np.ascontiguousarray(np.asarray(inputs["L0_attn_a_Wv"], f32)),
        "Wo": np.ascontiguousarray(np.asarray(inputs["L0_attn_a_Wo"], f32)),
        "bq": np.ascontiguousarray(np.asarray(inputs["L0_attn_a_bq"], f32)),
        "bk": np.ascontiguousarray(np.asarray(inputs["L0_attn_a_bk"], f32)),
        "bv": np.ascontiguousarray(np.asarray(inputs["L0_attn_a_bv"], f32)),
        "bo": np.ascontiguousarray(np.asarray(inputs["L0_attn_a_bo"], f32)),
        "Wa": np.ascontiguousarray(np.asarray(inputs["L0_attn_b_W_attn"], f32)),
        "ba": np.ascontiguousarray(np.asarray(inputs["L0_attn_b_b_attn"], f32)),
        "Wp": np.ascontiguousarray(np.asarray(inputs["L0_attn_b_W_proj"], f32)),
        "bp": np.ascontiguousarray(np.asarray(inputs["L0_attn_b_b_proj"], f32)),
        "w1T": np.ascontiguousarray(np.asarray(inputs["L0_ffn_w1_w"], f32).T),
        "w2T": np.ascontiguousarray(np.asarray(inputs["L0_ffn_w2_w"], f32).T),
        "h1T": np.ascontiguousarray(np.asarray(inputs["L0_ffn_h1_w"], f32).T),
        "h2T": np.ascontiguousarray(np.asarray(inputs["L0_ffn_h2_w"], f32).T),
        "w3T": np.ascontiguousarray(np.asarray(inputs["L0_ffn_w3_w"], f32).T),
        "s1c": np.ascontiguousarray(
            np.asarray(inputs["L0_ffn_w1_s"], f32).reshape(32, 128).T),
        "s2c": np.ascontiguousarray(
            np.asarray(inputs["L0_ffn_w2_s"], f32).reshape(32, 128).T),
        "h1c": np.ascontiguousarray(
            np.asarray(inputs["L0_ffn_h1_s"], f32).reshape(32, 128).T),
        "h2c": np.ascontiguousarray(
            np.asarray(inputs["L0_ffn_h2_s"], f32).reshape(32, 128).T),
        "s3r": np.ascontiguousarray(np.asarray(inputs["L0_ffn_w3_s"], f32)),
        "c_eye": np.eye(128, dtype=f32),
        "c_tri": np.tril(np.ones((32, 32), f32)).reshape(-1),
        "c_dmask": np.where(np.tril(np.ones((128, 128), bool)), 0.0,
                            -1e9).astype(f32),
        "c_ones": np.ones((512,), f32),
    }
    in_maps = []
    for b in range(8):
        m = dict(base)
        m["x"] = np.ascontiguousarray(x[b])
        in_maps.append(m)
    res = run_bass_kernel_spmd(nc, in_maps, core_ids=list(range(8)))
    out = np.stack([np.asarray(r["y"], dtype=f32) for r in res.results])
    if os.environ.get("KDBG_CAP"):
        np.save("/tmp/kdbg_cap.npy",
                np.stack([np.asarray(r["dbg"], dtype=f32) for r in res.results]))
    return out
